# version 01
# baseline (speedup 1.0000x reference)
import math
import numpy as np
import concourse.bass as bass
import concourse.mybir as mybir
from concourse.bass_utils import run_bass_kernel_spmd

DBG_GKB = False
DBG_SPF = 0
DBG_P2 = False
F32 = mybir.dt.float32
BF16 = mybir.dt.bfloat16
I32 = mybir.dt.int32
I8 = mybir.dt.int8
AF = mybir.ActivationFunctionType
ALU = mybir.AluOpType
AX = mybir.AxisListType

B = 8
D = 2048
H = 16
DK = 128
DV = 128
T = 2048
TOWN = 256
HALO = 128
CPAD = 3
TSC = HALO + TOWN
TIN = TSC + CPAD
NCH = TSC // 128
NCORES = 8
KD = H * DK
E = 8
FFN = 4096
NKV = 4
HD = 128

ENGINES = ("pe", "act", "dve", "pool", "sp")


class Prog:

    def __init__(self):
        self.ops = []
        self.res = {}
        self.nbarrier = 0

    def op(self, eng, emit, reads=(), writes=(), dma=False, tag="",
           strict=False):
        deps = set()
        for k in reads:
            e = self.res.get(k)
            if e is None:
                continue
            if e["w"] is not None:
                deps.add(e["w"])
        for k in writes:
            e = self.res.get(k)
            if e is None:
                continue
            if e["w"] is not None:
                deps.add(e["w"])
            deps.update(e["r"])
        opid = len(self.ops)
        self.ops.append(dict(eng=eng, emit=emit, deps=deps, dma=dma, tag=tag,
                             inc=False, cnt=None, strict=strict))
        for k in reads:
            self.res.setdefault(k, {"w": None, "r": set()})["r"].add(opid)
        for k in writes:
            self.res[k] = {"w": opid, "r": set()}
        return opid

    def barrier(self):
        last = {}
        for i, o in enumerate(self.ops):
            last[o["eng"]] = i
        deps = set(last.values())
        opid = len(self.ops)
        for eng in ENGINES:
            self.ops.append(dict(eng=eng, emit=None, deps=set(deps), dma=False,
                                 tag="barrier", inc=False, cnt=None,
                                 strict=False))
            deps = set(deps)
        self.res = {}
        self.nbarrier += 1

    def emit(self, nc, block, sems):
        ops = self.ops
        for o in ops:
            for d in o["deps"]:
                if ops[d]["eng"] != o["eng"]:
                    ops[d]["inc"] = True
        def real_deps(d, out, seen):
            if d in seen:
                return
            seen.add(d)
            if ops[d]["emit"] is None:
                for dd in ops[d]["deps"]:
                    real_deps(dd, out, seen)
            else:
                out.add(d)
        for o in ops:
            nd = set()
            seen = set()
            for d in o["deps"]:
                real_deps(d, nd, seen)
            o["deps"] = nd
        for o in ops:
            o["inc"] = False
        for o in ops:
            for d in o["deps"]:
                if ops[d]["eng"] != o["eng"] or ops[d]["dma"] or o["strict"]:
                    ops[d]["inc"] = True
        for o in ops:
            if o["dma"]:
                o["inc"] = True
        cnt = {e: 0 for e in ENGINES}
        for o in ops:
            if o["emit"] is None:
                o["cnt"] = None
                continue
            if o["inc"]:
                cnt[o["eng"]] += 16 if o["dma"] else 1
            o["cnt"] = cnt[o["eng"]]
        self.final_counts = dict(cnt)
        per_eng = {e: [] for e in ENGINES}
        for o in ops:
            per_eng[o["eng"]].append(o)
        for eng in ENGINES:
            waited = {e: 0 for e in ENGINES}

            def body(e, eng=eng, waited=waited):
                for o in per_eng[eng]:
                    need = {}
                    for d in o["deps"]:
                        od = ops[d]
                        if od["eng"] == eng and not od["dma"] and not o["strict"]:
                            continue
                        assert od["inc"] and od["cnt"] is not None, (
                            f"dep without inc: {od['tag']} <- {o['tag']}")
                        need[od["eng"]] = max(need.get(od["eng"], 0), od["cnt"])
                    for seng, v in sorted(need.items()):
                        if waited[seng] < v:
                            e.wait_ge(sems[seng], v)
                            waited[seng] = v
                    if o["emit"] is None:
                        continue
                    instr = o["emit"](e)
                    if o["inc"]:
                        instr.then_inc(sems[eng], 16 if o["dma"] else 1)
            getattr(block, {"pe": "tensor", "act": "scalar", "dve": "vector",
                            "pool": "gpsimd", "sp": "sync"}[eng])(body)

    def final_wait(self, nc, block, sems):
        cnt = self.final_counts

        def body(e):
            for eng in ENGINES:
                if cnt[eng]:
                    e.wait_ge(sems[eng], cnt[eng])
        block.sync(body)



def ceil_div(a, b):
    return -(-a // b)


class KBuild:

    def __init__(self, nc):
        self.nc = nc
        self.p = Prog()
        self.guards = {}
        self.name_i = 0

    def sbuf(self, name, shape, dtype):
        self.name_i += 1
        g = self.nc.sbuf_tensor(f"{name}__{self.name_i}", shape, dtype)
        t = g.__enter__()
        self.guards[name] = g
        return t

    def free(self, *names):
        self.p.barrier()
        for n in names:
            self.guards.pop(n).__exit__(None, None, None)

    def psum(self, name, shape, dtype=F32):
        g = self.nc.psum_tensor(name, shape, dtype)
        t = g.__enter__()
        self.guards[name] = g
        return t


def build_setup(kb, dram):
    nc, p = kb.nc, kb.p
    c = {}
    c["ones_col"] = kb.sbuf("ones_col", [128, 2], BF16)
    c["ones_row"] = kb.sbuf("ones_row", [1, 128], BF16)
    c["qscale_row"] = kb.sbuf("qscale_row", [1, 128], BF16)
    c["eye"] = kb.sbuf("eye", [128, 128], BF16)
    c["eyef"] = kb.sbuf("eyef", [128, 128], F32)
    c["negbig"] = kb.sbuf("negbig", [128, 128], F32)
    for name in ("m8T_s", "m8T_i", "m8_i", "mT_16", "mT_32", "mT_64", "mT_128"):
        c[name] = kb.sbuf(name, [128, 128], I8)

    for nm in ("dtb", "gb2", "c_ln1w", "c_ln1b", "c_ln2w", "c_ln2b"):
        c[nm] = kb.sbuf("sb_" + nm, [128, 16], F32)
    c["onw"] = kb.sbuf("onw", [128, 1], F32)
    for nm in ("qcw", "kcw", "vcw"):
        c[nm] = kb.sbuf(nm, [128, 64], F32)
    c["rates"] = kb.sbuf("rates", [128, 16], F32)
    c["eps5"] = kb.sbuf("eps5", [128, 1], F32)
    c["eps6"] = kb.sbuf("eps6", [128, 1], F32)
    rowst = {}
    for nm in ("dtb", "gb2", "c_ln1w", "c_ln1b", "c_ln2w", "c_ln2b"):
        rowst[nm] = kb.sbuf(nm + "_row", [16, 128], F32)
    onw_row = kb.sbuf("onw_row", [1, 128], F32)
    alog_row = kb.sbuf("alog_row", [1, 16], F32)
    ratec = kb.sbuf("rate_row", [1, 16], BF16)
    onesf = kb.sbuf("mk_ones", [128, 128], F32)
    ci = kb.sbuf("mk_ci", [128, 128], I32)
    pi = kb.sbuf("mk_pi", [128, 128], I32)
    t1 = kb.sbuf("mk_t1", [128, 128], I32)
    t2 = kb.sbuf("mk_t2", [128, 128], I32)
    t3 = kb.sbuf("mk_t3", [128, 128], I32)
    kb.setup_free = ["mk_t3", "mk_t2", "mk_t1", "mk_pi", "mk_ci",
                     "mk_ones"]

    p.op("pool", lambda e: e.memset(c["ones_col"][:], 1.0), writes=["ones_col"])
    p.op("pool", lambda e: e.memset(c["ones_row"][:], 1.0), writes=["ones_row"])
    p.op("pool", lambda e: e.memset(c["qscale_row"][:], float(DK) ** -0.5),
         writes=["qscale_row"])
    p.op("pool", lambda e: e.memset(c["negbig"][:], -1e9), writes=["negbig"])
    p.op("pool", lambda e: e.memset(c["eps5"][:], 1e-5), writes=["eps5"])
    p.op("pool", lambda e: e.memset(c["eps6"][:], 1e-6), writes=["eps6"])
    p.op("pool", lambda e: e.memset(onesf[:], 1.0), writes=["mk_ones"])
    p.op("pool", lambda e: e.affine_select(c["eyef"][:], onesf[:], [[1, 128]],
                                           ALU.is_equal, 0.0, base=0,
                                           channel_multiplier=-1),
         reads=["mk_ones"], writes=["eyef"])
    p.op("dve", lambda e: e.tensor_copy(c["eye"][:], c["eyef"][:]),
         reads=["eyef"], writes=["eye"])
    p.op("pool", lambda e: e.iota(ci[:], [[1, 128]], base=0, channel_multiplier=0),
         writes=["mk_ci"])
    p.op("pool", lambda e: e.iota(pi[:], [[1, 128]], base=0, channel_multiplier=1),
         writes=["mk_pi"])
    p.op("dve", lambda e: e.tensor_tensor(pi[:], pi[:], ci[:], ALU.subtract),
         reads=["mk_ci", "mk_pi"], writes=["mk_pi"])

    def build_same_block(lg2, key):
        p.op("dve", lambda e: e.tensor_scalar(t1[:], pi[:], lg2, None,
                                              ALU.arith_shift_right),
             reads=["mk_pi"], writes=["mk_t1"])
        p.op("dve", lambda e: e.tensor_scalar(t2[:], ci[:], lg2, None,
                                              ALU.arith_shift_right),
             reads=["mk_ci"], writes=["mk_t2"])
        p.op("dve", lambda e: e.tensor_tensor(t3[:], t1[:], t2[:], ALU.is_equal),
             reads=["mk_t1", "mk_t2"], writes=[key])

    build_same_block(3, "mk_t3")
    for cmp_op, dst in ((ALU.is_gt, "m8T_s"), (ALU.is_ge, "m8T_i")):
        p.op("dve", lambda e, cmp_op=cmp_op: e.tensor_tensor(
            t1[:], ci[:], pi[:], cmp_op),
            reads=["mk_ci", "mk_pi"], writes=["mk_t1"])
        p.op("dve", lambda e: e.tensor_tensor(t2[:], t1[:], t3[:], ALU.mult),
             reads=["mk_t1", "mk_t3"], writes=["mk_t2"])
        p.op("dve", lambda e, dst=dst: e.tensor_copy(c[dst][:], t2[:]),
             reads=["mk_t2"], writes=[dst])
    p.op("dve", lambda e: e.tensor_tensor(t1[:], pi[:], ci[:], ALU.is_ge),
         reads=["mk_ci", "mk_pi"], writes=["mk_t1"])
    p.op("dve", lambda e: e.tensor_tensor(t2[:], t1[:], t3[:], ALU.mult),
         reads=["mk_t1", "mk_t3"], writes=["mk_t2"])
    p.op("dve", lambda e: e.tensor_copy(c["m8_i"][:], t2[:]),
         reads=["mk_t2"], writes=["m8_i"])
    for lg2, name in ((4, "mT_16"), (5, "mT_32"), (6, "mT_64"), (7, "mT_128")):
        build_same_block(lg2, "mk_t3b")
        half = 1 << (lg2 - 1)
        mod = (1 << lg2) - 1
        p.op("dve", lambda e, mod=mod: e.tensor_scalar(t1[:], ci[:], mod, None,
                                                       ALU.bitwise_and),
             reads=["mk_ci"], writes=["mk_t1"])
        p.op("dve", lambda e, half=half: e.tensor_scalar(t1[:], t1[:], half,
                                                         None, ALU.is_ge),
             reads=["mk_t1"], writes=["mk_t1"])
        p.op("dve", lambda e, mod=mod: e.tensor_scalar(t2[:], pi[:], mod, None,
                                                       ALU.bitwise_and),
             reads=["mk_pi"], writes=["mk_t2"])
        p.op("dve", lambda e, half=half: e.tensor_scalar(t2[:], t2[:], half,
                                                         None, ALU.is_lt),
             reads=["mk_t2"], writes=["mk_t2"])
        p.op("dve", lambda e: e.tensor_tensor(t1[:], t1[:], t2[:], ALU.mult),
             reads=["mk_t1", "mk_t2"], writes=["mk_t1"])
        p.op("dve", lambda e: e.tensor_tensor(t1[:], t1[:], t3[:], ALU.mult),
             reads=["mk_t1", "mk_t3b"], writes=["mk_t1"])
        p.op("dve", lambda e, name=name: e.tensor_copy(c[name][:], t1[:]),
             reads=["mk_t1"], writes=[name])

    ps_t = kb.ps["sm"]
    for nm, key in (("dtb", "dt_bias"), ("gb2", "g_b2"),
                    ("c_ln1w", "ln1w"), ("c_ln1b", "ln1b"),
                    ("c_ln2w", "ln2w"), ("c_ln2b", "ln2b")):
        row = rowst[nm]
        p.op("sp", lambda e, row=row, key=key: e.dma_start(
            out=row[:], in_=dram[key].ap().rearrange("(a b) -> a b", b=128)),
            dma=True, writes=[nm + "_row"])
        p.op("pe", lambda e, row=row: e.transpose(ps_t[:, 0:16], row[:],
                                                  c["eyef"][0:16, 0:16]),
             reads=[nm + "_row", "eyef"], writes=["ps_setup"], tag=nm)
        p.op("dve", lambda e, nm=nm: e.tensor_copy(c[nm][:], ps_t[:, 0:16]),
             reads=["ps_setup"], writes=[nm])
    p.op("sp", lambda e: e.dma_start(
        out=onw_row[:], in_=dram["o_norm_w"].ap().rearrange("(a b) -> a b", a=1)),
        dma=True, writes=["onw_row"])
    p.op("pe", lambda e: e.transpose(ps_t[:, 0:1], onw_row[:],
                                     c["eyef"][0:1, 0:1]),
         reads=["onw_row", "eyef"], writes=["ps_setup"])
    p.op("dve", lambda e: e.tensor_copy(c["onw"][:], ps_t[:, 0:1]),
         reads=["ps_setup"], writes=["onw"])
    for nm, key in (("qcw", "q_conv_w"), ("kcw", "k_conv_w"), ("vcw", "v_conv_w")):
        for mm_ in range(16):
            p.op("sp", lambda e, nm=nm, key=key, mm_=mm_: e.dma_start(
                out=c[nm][:, mm_ * 4:(mm_ + 1) * 4],
                in_=bass.AP(dram[key], mm_ * 512, [[4, 128], [1, 4]])),
                dma=True, writes=[nm])
    p.op("sp", lambda e: e.dma_start(
        out=alog_row[:], in_=dram["A_log"].ap().rearrange("(a b) -> a b", a=1)),
        dma=True, writes=["alog_row"])
    p.op("act", lambda e: e.activation(ratec[:], alog_row[:], AF.Exp),
         reads=["alog_row"], writes=["rate_row"])
    p.op("pe", lambda e: e.matmul(ps_t[:, 0:16], c["ones_row"][:], ratec[:],
                                  start=True, stop=True),
         reads=["rate_row", "ones_row"], writes=["ps_setup"])
    p.op("dve", lambda e: e.tensor_copy(c["rates"][:], ps_t[:, 0:16]),
         reads=["ps_setup"], writes=["rates"])
    return c



def build_phase1a(kb, c, dram, scr):
    nc, p = kb.nc, kb.p
    kb.free(*kb.setup_free)
    c["rmask"] = kb.sbuf("rmask", [128, H * TSC], BF16)
    p.op("pool", lambda e: e.memset(c["rmask"][:], 1.0), writes=["rmask"])
    p.op("pool", lambda e: e.memset(
        bass.AP(c["rmask"], 0, [[H * TSC, 128], [128, H * NCH]]), 0.0),
        reads=["rmask"], writes=["rmask"])


    fw1 = kb.sbuf("fw1", [128, 16 * 128], BF16)
    kb.dbg_fw1 = fw1
    gw1 = kb.sbuf("gw1", [128, 16 * 128], BF16)
    fw2 = kb.sbuf("fw2", [128, 16 * 128], BF16)
    bw = kb.sbuf("bw", [128, 16 * 16], BF16)
    for k in range(16):
        p.op("pool", lambda e, k=k: e.dma_start(
            out=fw1[:, k * 128:(k + 1) * 128],
            in_=bass.AP(dram["f_w1"], k * 128 * 128, [[128, 128], [1, 128]])),
            dma=True, writes=["fw1"])
        p.op("pool", lambda e, k=k: e.dma_start(
            out=gw1[:, k * 128:(k + 1) * 128],
            in_=bass.AP(dram["g_w1"], k * 128 * 128, [[128, 128], [1, 128]])),
            dma=True, writes=["gw1"])
        p.op("pool", lambda e, k=k: e.dma_start(
            out=fw2[:, k * 128:(k + 1) * 128],
            in_=bass.AP(dram["f_w2"], k * 128, [[2048, 128], [1, 128]])),
            dma=True, writes=["fw2"])
        p.op("pool", lambda e, k=k: e.dma_start(
            out=bw[:, k * 16:(k + 1) * 16],
            in_=bass.AP(dram["b_w"], k * 128 * 16, [[16, 128], [1, 16]])),
            dma=True, writes=["bw"])

    gmid2 = kb.sbuf("gmid2", [128, 8 * TIN], BF16)
    beta_f = kb.sbuf("beta_f", [16, TIN], F32)
    c["beta_bf"] = kb.sbuf("beta_bf", [16, 8 * TSC], BF16)
    c["betaT"] = kb.sbuf("betaT", [128, 8 * NCH * 16], F32)
    c["gmid2"] = gmid2
    h_ar = kb.sbuf("h_ar", [128, 2 * 16 * TIN], BF16)
    x_ar = kb.sbuf("x_ar", [128, 16 * TIN], F32)
    xbf = kb.sbuf("xbf", [128, 16 * TIN], BF16)
    w_ar = kb.sbuf("w_ar", [128, 16 * 2048], BF16)
    gmid1 = kb.sbuf("gmid1", [128, 8 * TIN], BF16)
    kb.dbg_gmid1 = gmid1
    gkb = xbf
    Gb = x_ar
    rows = kb.sbuf("rows", [1, 6 * 512], F32)
    rows_bf = kb.sbuf("rows_bf", [1, 2 * 512], BF16)
    ql = [kb.sbuf(f"ql{i}", [128, TIN], BF16) for i in range(2)]
    cv = [kb.sbuf(f"cv{i}", [128, TSC], BF16) for i in range(2)]
    cv2 = [kb.sbuf(f"cv2{i}", [128, TSC], BF16) for i in range(2)]
    sv = [kb.sbuf(f"sv{i}", [128, TSC], BF16) for i in range(2)]
    sq = kb.sbuf("sqt", [128, TSC], BF16)
    spf = kb.sbuf("spf", [128, TIN], F32)
    kb.dbg_spf = spf

    ps_a, ps_b, ps_bc, ps_bc2 = kb.ps["a"], kb.ps["b"], kb.ps["bc"], kb.ps["bc2"]
    ps_sa, ps_sb, ps_l2, ps_sm = kb.ps["sa"], kb.ps["sb"], kb.ps["l2"], kb.ps["sm"]

    def h_sl(b, k):
        off = ((b % 2) * 16 + k) * TIN
        return h_ar[:, off:off + TIN]

    for grp in range(4):
        bs = range(grp * 2, grp * 2 + 2)
        for b in bs:
            for k in range(16):
                p.op("pool", lambda e, b=b, k=k: e.dma_start(
                    out=xbf[:, k * TIN:(k + 1) * TIN],
                    in_=bass.AP(dram["xin"], (b * 16 + k) * 128 * TIN,
                                [[TIN, 128], [1, TIN]])),
                    dma=True, writes=["xbf"], tag=f"xbf{b}_{k}")
                p.op("sp", lambda e, b=b, k=k: e.dma_start(
                    out=x_ar[:, k * TIN:(k + 1) * TIN],
                    in_=bass.AP(dram["xin"], (b * 16 + k) * 128 * TIN,
                                [[TIN, 128], [1, TIN]])),
                    dma=True, writes=["x_ar"], tag=f"x{b}_{k}")
            for k in range(16):
                p.op("pe", lambda e, k=k: e.matmul(
                    ps_sa[0:1, 0:TIN], c["ones_col"][:, 0:1],
                    xbf[:, k * TIN:(k + 1) * TIN], start=(k == 0), stop=(k == 15)),
                    reads=["xbf", "ones_col"], writes=["ps_sa"])
            p.op("act", lambda e: e.activation(xbf[:], xbf[:], AF.Square),
                 reads=["xbf"], writes=["xbf"])
            for k in range(16):
                p.op("pe", lambda e, k=k: e.matmul(
                    ps_sb[0:1, 0:TIN], c["ones_col"][:, 0:1],
                    xbf[:, k * TIN:(k + 1) * TIN], start=(k == 0), stop=(k == 15)),
                    reads=["xbf", "ones_col"], writes=["ps_sb"])
            R = lambda i: rows[0:1, i * 512:i * 512 + TIN]
            p.op("dve", lambda e: e.tensor_scalar(R(0), ps_sa[0:1, 0:TIN],
                                                  1.0 / D, None, ALU.mult),
                 reads=["ps_sa"], writes=["rows"])
            p.op("act", lambda e: e.activation(R(1), R(0), AF.Square),
                 reads=["rows"], writes=["rows"])
            p.op("dve", lambda e: e.scalar_tensor_tensor(
                R(2), ps_sb[0:1, 0:TIN], 1.0 / D, R(1), ALU.mult, ALU.subtract),
                reads=["ps_sb", "rows"], writes=["rows"])
            p.op("act", lambda e: e.activation(R(3), R(2), AF.Sqrt,
                                               bias=c["eps5"][0:1, 0:1]),
                 reads=["rows", "eps5"], writes=["rows"])
            p.op("dve", lambda e: e.reciprocal(R(4), R(3)),
                 reads=["rows"], writes=["rows"])
            p.op("dve", lambda e: e.tensor_tensor(R(5), R(0), R(4), ALU.mult),
                 reads=["rows"], writes=["rows"])
            p.op("dve", lambda e: e.tensor_copy(rows_bf[0:1, 0:TIN], R(4)),
                 reads=["rows"], writes=["rows_bf"])
            p.op("dve", lambda e: e.tensor_copy(rows_bf[0:1, 512:512 + TIN], R(5)),
                 reads=["rows"], writes=["rows_bf"])
            p.op("pe", lambda e: e.matmul(ps_bc[:, 0:TIN], c["ones_row"][:],
                                          rows_bf[0:1, 0:TIN], start=True, stop=True),
                 reads=["rows_bf", "ones_row"], writes=["ps_bc"])
            p.op("pe", lambda e: e.matmul(ps_bc2[:, 0:TIN], c["ones_row"][:],
                                          rows_bf[0:1, 512:512 + TIN],
                                          start=True, stop=True),
                 reads=["rows_bf", "ones_row"], writes=["ps_bc2"])
            p.op("dve", lambda e: e.tensor_tensor(
                bass.AP(x_ar, 0, [[16 * TIN, 128], [TIN, 16], [1, TIN]]),
                bass.AP(x_ar, 0, [[16 * TIN, 128], [TIN, 16], [1, TIN]]),
                bass.AP(ps_bc, 0, [[512, 128], [0, 16], [1, TIN]]), ALU.mult),
                reads=["x_ar", "ps_bc"], writes=["x_ar"])
            p.op("dve", lambda e, b=b: e.tensor_tensor(
                bass.AP(h_ar, (b % 2) * 16 * TIN, [[2 * 16 * TIN, 128], [TIN, 16], [1, TIN]]),
                bass.AP(x_ar, 0, [[16 * TIN, 128], [TIN, 16], [1, TIN]]),
                bass.AP(ps_bc2, 0, [[512, 128], [0, 16], [1, TIN]]), ALU.subtract),
                reads=["x_ar", "ps_bc2"], writes=[f"h{b}"], tag=f"h{b}")

        for S, wkey, cwkey, scrkey in (("q", "q_w", "qcw", "qscr"),
                                       ("k", "k_w", "kcw", "kscr"),
                                       ("v", "v_w", "vcw", "vscr")):
            for k in range(16):
                p.op("pool", lambda e, k=k, wkey=wkey: e.dma_start(
                    out=w_ar[:, k * 2048:(k + 1) * 2048],
                    in_=bass.AP(dram[wkey], k * 128 * 2048, [[2048, 128], [1, 2048]])),
                    dma=True, writes=[f"w_{k}"], tag=f"W{S}{k}")
            for b in bs:
                for m in range(16):
                    ii = (b * 16 + m) % 2
                    ps = ps_a if ii == 0 else ps_b
                    pskey = "ps_a" if ii == 0 else "ps_b"
                    for k in range(16):
                        p.op("pe", lambda e, k=k, m=m, b=b, ps=ps: e.matmul(
                            ps[:, 0:TIN],
                            w_ar[:, k * 2048 + m * 128:k * 2048 + m * 128 + 128],
                            h_sl(b, k), start=(k == 0), stop=(k == 15)),
                            reads=[f"w_{k}", f"h{b}"], writes=[pskey])
                    p.op("dve", lambda e, ps=ps, ii=ii: e.tensor_copy(
                        ql[ii][:], ps[:, 0:TIN]),
                        reads=[pskey], writes=[f"ql{ii}"])
                    def cwv(m, i, cwkey):
                        return bass.AP(c[cwkey], m * 4 + i, [[64, 128], [0, TSC]])
                    p.op("pool", lambda e, ii=ii, m=m, cwkey=cwkey: e.tensor_tensor(
                        cv[ii][:], ql[ii][:, 3:TIN], cwv(m, 3, cwkey), ALU.mult),
                        reads=[f"ql{ii}", cwkey], writes=[f"cv{ii}"])
                    for i in (2, 1, 0):
                        p.op("pool", lambda e, ii=ii, m=m, i=i, cwkey=cwkey:
                             e.tensor_tensor(
                                 cv2[ii][:], ql[ii][:, i:TSC + i],
                                 cwv(m, i, cwkey), ALU.mult),
                             reads=[f"ql{ii}", cwkey], writes=[f"cv2{ii}"])
                        p.op("pool", lambda e, ii=ii: e.tensor_tensor(
                            cv[ii][:], cv[ii][:], cv2[ii][:], ALU.add),
                            reads=[f"cv{ii}", f"cv2{ii}"], writes=[f"cv{ii}"])
                    p.op("act", lambda e, ii=ii: e.activation(
                        sv[ii][:], cv[ii][:], AF.Silu),
                        reads=[f"cv{ii}"], writes=[f"sv{ii}"])
                    if S in ("q", "k"):
                        p.op("act", lambda e, ii=ii: e.activation(
                            sq[:], sv[ii][:], AF.Square),
                            reads=[f"sv{ii}"], writes=["sqt"])
                        p.op("pe", lambda e: e.matmul(
                            ps_l2[0:1, 0:TSC], c["ones_col"][:, 0:1], sq[:],
                            start=True, stop=True),
                            reads=["sqt", "ones_col"], writes=["ps_l2"])
                        p.op("act", lambda e: e.activation(
                            rows[0:1, 3 * 512:3 * 512 + TSC], ps_l2[0:1, 0:TSC],
                            AF.Sqrt, bias=c["eps6"][0:1, 0:1]),
                            reads=["ps_l2", "eps6"], writes=["rows"])
                        p.op("dve", lambda e: e.reciprocal(
                            rows[0:1, 4 * 512:4 * 512 + TSC],
                            rows[0:1, 3 * 512:3 * 512 + TSC]),
                            reads=["rows"], writes=["rows"])
                        p.op("dve", lambda e: e.tensor_copy(
                            rows_bf[0:1, 0:TSC], rows[0:1, 4 * 512:4 * 512 + TSC]),
                            reads=["rows"], writes=["rows_bf"])
                        lr = "qscale_row" if S == "q" else "ones_row"
                        p.op("pe", lambda e, lr=lr: e.matmul(
                            ps_bc[:, 0:TSC], c[lr][:], rows_bf[0:1, 0:TSC],
                            start=True, stop=True),
                            reads=["rows_bf", lr], writes=["ps_bc"])
                        p.op("dve", lambda e, ii=ii: e.tensor_tensor(
                            sv[ii][:], sv[ii][:], ps_bc[:, 0:TSC], ALU.mult),
                            reads=[f"sv{ii}", "ps_bc"], writes=[f"sv{ii}"])
                    p.op("sp", lambda e, ii=ii, b=b, m=m, scrkey=scrkey:
                         e.dma_start(out=bass.AP(scr[scrkey],
                                                 (b * 16 + m) * 128 * TSC,
                                                 [[TSC, 128], [1, TSC]]),
                                     in_=sv[ii][:]),
                         dma=True, reads=[f"sv{ii}"], tag=f"st{S}{b}_{m}")

        for b in bs:
            for mid, wsb, wk in ((gmid1, fw1, "fw1"), (gmid2, gw1, "gw1")):
                ii = 0 if wk == "fw1" else 1
                ps = ps_a if ii == 0 else ps_b
                pskey = "ps_a" if ii == 0 else "ps_b"
                for k in range(16):
                    p.op("pe", lambda e, k=k, b=b, wsb=wsb, ps=ps: e.matmul(
                        ps[:, 0:TIN], wsb[:, k * 128:(k + 1) * 128], h_sl(b, k),
                        start=(k == 0), stop=(k == 15)),
                        reads=[wk, f"h{b}"], writes=[pskey])
                p.op("dve", lambda e, mid=mid, b=b, ps=ps: e.tensor_copy(
                    mid[:, b * TIN:(b + 1) * TIN], ps[:, 0:TIN]),
                    reads=[pskey], writes=[f"{wk}mid{b}"])
            for k in range(16):
                p.op("pe", lambda e, k=k, b=b: e.matmul(
                    ps_bc2[0:16, 0:TIN], bw[:, k * 16:(k + 1) * 16], h_sl(b, k),
                    start=(k == 0), stop=(k == 15)),
                    reads=["bw", f"h{b}"], writes=["ps_bc2"])
            p.op("act", lambda e: e.activation(beta_f[:], ps_bc2[0:16, 0:TIN],
                                               AF.Sigmoid),
                 reads=["ps_bc2"], writes=["beta_f"])
            p.op("dve", lambda e, b=b: e.tensor_copy(
                c["beta_bf"][:, b * TSC:(b + 1) * TSC], beta_f[:, CPAD:TIN]),
                reads=["beta_f"], writes=["beta_bf"])
            for ci in range(NCH):
                p.op("pe", lambda e, ci=ci: e.transpose(
                    ps_sm[:, 0:16], beta_f[0:16, CPAD + ci * 128:CPAD + ci * 128 + 128],
                    c["eyef"][0:16, 0:16]),
                    reads=["beta_f", "eyef"], writes=["ps_sm"])
                p.op("dve", lambda e, b=b, ci=ci: e.tensor_copy(
                    c["betaT"][:, (b * NCH + ci) * 16:(b * NCH + ci) * 16 + 16],
                    ps_sm[:, 0:16]),
                    reads=["ps_sm"], writes=["betaT"])
            for m in range(16):
                ii = m % 2
                ps = ps_a if ii == 0 else ps_b
                pskey = "ps_a" if ii == 0 else "ps_b"
                p.op("pe", lambda e, m=m, b=b, ps=ps: e.matmul(
                    ps[:, 0:TIN], fw2[:, m * 128:(m + 1) * 128],
                    gmid1[:, b * TIN:(b + 1) * TIN], start=True, stop=True),
                    reads=["fw2", f"fw1mid{b}"], writes=[pskey])
                p.op("act", lambda e, m=m, ps=ps: e.activation(
                    spf[:], ps[:, 0:TIN],
                    AF.Identity if DBG_SPF == 1 else AF.Exp,
                    bias=c["dtb"][:, m:m + 1]),
                    reads=[pskey, "dtb"], writes=["spf"])
                if DBG_SPF != 1:
                    p.op("act", lambda e: e.activation(spf[:], spf[:], AF.Ln,
                                                       bias=1.0),
                         reads=["spf"], writes=["spf"])
                p.op("dve", lambda e, m=m: e.tensor_scalar(
                    gkb[:, m * TSC:(m + 1) * TSC], spf[:, CPAD:TIN],
                    c["rates"][:, m:m + 1], -1.0, ALU.mult, ALU.mult),
                    reads=["spf", "rates"], writes=["xbf"], strict=True)
            p.op("dve", lambda e: e.tensor_tensor_scan(
                bass.AP(Gb, 0, [[16 * TIN, 128], [1, H * TSC]]),
                c["rmask"][:], gkb[:, 0:H * TSC], 0.0, ALU.mult, ALU.add),
                reads=["xbf", "rmask"], writes=["x_ar"])
            for hh_ in range(H):
                src_ap = (bass.AP(gkb, hh_ * TSC, [[16 * TIN, 128], [1, TSC]])
                          if DBG_GKB else
                          bass.AP(Gb, hh_ * TSC, [[16 * TIN, 128], [1, TSC]]))
                eng_g = "pool" if DBG_GKB else "sp"
                p.op(eng_g, lambda e, b=b, hh_=hh_, src_ap=src_ap: e.dma_start(
                    out=bass.AP(scr["gscr"], (b * H + hh_) * 128 * TSC,
                                [[TSC, 128], [1, TSC]]),
                    in_=src_ap),
                    dma=True, reads=["x_ar", "xbf"], tag=f"gst{b}_{hh_}")
    return dict(h_ar=h_ar, gkb=gkb, Gb=Gb, gmid1=gmid1, gmid2=gmid2,
                beta_f=beta_f, rows=rows, rows_bf=rows_bf, spf=spf,
                fw2=fw2, ps=dict(a=ps_a, b=ps_b, bc=ps_bc, bc2=ps_bc2,
                                 sa=ps_sa, sb=ps_sb, l2=ps_l2, sm=ps_sm))


INPUT_SPECS = [
    ("xin", [B, D, TIN]), ("q_w", [D, KD]), ("k_w", [D, KD]), ("v_w", [D, KD]),
    ("o_w", [KD, D]), ("f_w1", [D, 128]), ("f_w2", [128, KD]),
    ("g_w1", [D, 128]), ("g_w2", [128, KD]), ("b_w", [D, H]),
    ("q_conv_w", [KD, 4]), ("k_conv_w", [KD, 4]), ("v_conv_w", [KD, 4]),
    ("A_log", [H]), ("dt_bias", [KD]), ("g_b2", [KD]), ("o_norm_w", [DV]),
    ("ln1w", [D]), ("ln1b", [D]), ("ln2w", [D]), ("ln2b", [D]),
    ("qkvw", [D, 3072]), ("projw", [D, D]), ("rw", [D, E]),
    ("mw1", [E, D, FFN]), ("mw2", [E, FFN, D]),
]


def build_nc(stage="full"):
    nc = bass.Bass("TRN2", target_bir_lowering=False)
    dram = {}
    for name, shape in INPUT_SPECS:
        dram[name] = nc.dram_tensor(name, shape, F32, kind="ExternalInput")
    scr = {}
    k1a = "ExternalOutput" if stage == "1a" else "Internal"
    scr["qscr"] = nc.dram_tensor("qscr", [B, H, 128, TSC], BF16, kind=k1a)
    scr["kscr"] = nc.dram_tensor("kscr", [B, H, 128, TSC], BF16, kind=k1a)
    scr["vscr"] = nc.dram_tensor("vscr", [B, H, 128, TSC], BF16, kind=k1a)
    scr["gscr"] = nc.dram_tensor("gscr", [B, H, 128, TSC], F32, kind=k1a)
    k1 = "ExternalOutput" if stage in ("1a", "p1") else "Internal"
    scr["x1scr"] = nc.dram_tensor("x1scr", [B, 16, 128, TOWN], F32, kind=k1)
    if stage == "full":
        scr["outfm"] = nc.dram_tensor("outfm", [16, 128, 2 * 1024], F32,
                                      kind="ExternalOutput")
        scr["mw1b"] = nc.dram_tensor("mw1b", [E * 16 * 128, FFN], BF16)
        scr["mw2b"] = nc.dram_tensor("mw2b", [E * 32 * 128, D], BF16)
        scr["qkvwb"] = nc.dram_tensor("qkvwb", [D, 3072], BF16)
        scr["projwb"] = nc.dram_tensor("projwb", [D, D], BF16)

    kb = KBuild(nc)
    p = kb.p
    kb.ps = dict(
        a=kb.psum("ps_a", [128, 512]), b=kb.psum("ps_b", [128, 512]),
        bc=kb.psum("ps_bc", [128, 512]), bc2=kb.psum("ps_bc2", [128, 512]),
        sa=kb.psum("ps_sa", [128, 512]), sb=kb.psum("ps_sb", [128, 512]),
        l2=kb.psum("ps_l2", [128, 512]), sm=kb.psum("ps_sm", [128, 128]))
    c = build_setup(kb, dram)
    build_phase1a(kb, c, dram, scr)
    if stage == "1a":
        bout = nc.dram_tensor("beta_out", [16, 8 * TSC], F32, kind="ExternalOutput")
        btout = nc.dram_tensor("betaT_out", [128, 8 * NCH * 16], F32,
                               kind="ExternalOutput")
        p.op("pool", lambda e: e.dma_start(out=bout.ap(), in_=c["beta_bf"][:]),
             dma=True, reads=["beta_bf"])
        rout = nc.dram_tensor("rates_out", [128, 16], F32, kind="ExternalOutput")
        sout = nc.dram_tensor("spf_out", [128, TIN], F32, kind="ExternalOutput")
        dout = nc.dram_tensor("dtb_out", [128, 16], F32, kind="ExternalOutput")
        p.op("sp", lambda e: e.dma_start(out=rout.ap(), in_=c["rates"][:]),
             dma=True, reads=["rates"])
        p.op("sp", lambda e: e.dma_start(out=sout.ap(), in_=kb.dbg_spf[:]),
             dma=True, reads=["spf"])
        p.op("sp", lambda e: e.dma_start(out=dout.ap(), in_=c["dtb"][:]),
             dma=True, reads=["dtb"])
        gmout = nc.dram_tensor("gmid_out", [128, 8 * TIN], F32, kind="ExternalOutput")
        p.op("pool", lambda e: e.dma_start(out=gmout.ap(), in_=kb.dbg_gmid1[:]),
             dma=True, reads=[f"fw1mid{b}" for b in range(8)])
        fwout = nc.dram_tensor("fw1_out", [128, 2048], F32, kind="ExternalOutput")
        p.op("pool", lambda e: e.dma_start(out=fwout.ap(), in_=kb.dbg_fw1[:]),
             dma=True, reads=["fw1"])
        p.op("sp", lambda e: e.dma_start(out=btout.ap(), in_=c["betaT"][:]),
             dma=True, reads=["betaT"])
    elif stage in ("p1", "full"):
        p.barrier()
        build_phase1bc(kb, c, dram, scr)
        if stage == "full":
            p.barrier()
            build_phase2(kb, c, dram, scr)

    sems_cm = [nc.semaphore(f"sem_{e}") for e in ENGINES]
    sems = {}
    for eng, cm in zip(ENGINES, sems_cm):
        sems[eng] = cm.__enter__()
    with nc.Block() as block:
        p.emit(nc, block, sems)
        p.final_wait(nc, block, sems)
    return nc


def host_inputs(inputs, core):
    x = inputs["L0_attn_hidden_states"]
    t0 = 256 * core - (HALO + CPAD)
    if t0 >= 0:
        xs = x[:, t0:256 * core + TOWN, :]
    else:
        pad = np.zeros((B, -t0, D), np.float32)
        xs = np.concatenate([pad, x[:, :256 * core + TOWN, :]], axis=1)
    m = {
        "xin": np.ascontiguousarray(xs.transpose(0, 2, 1)),
        "q_w": inputs["L0_attn_q_w"], "k_w": inputs["L0_attn_k_w"],
        "v_w": inputs["L0_attn_v_w"], "o_w": inputs["L0_attn_o_w"],
        "f_w1": inputs["L0_attn_f_w1"], "f_w2": inputs["L0_attn_f_w2"],
        "g_w1": inputs["L0_attn_g_w1"], "g_w2": inputs["L0_attn_g_w2"],
        "b_w": inputs["L0_attn_b_w"],
        "q_conv_w": inputs["L0_attn_q_conv_w"],
        "k_conv_w": inputs["L0_attn_k_conv_w"],
        "v_conv_w": inputs["L0_attn_v_conv_w"],
        "A_log": inputs["L0_attn_A_log"], "dt_bias": inputs["L0_attn_dt_bias"],
        "g_b2": inputs["L0_attn_g_b2"], "o_norm_w": inputs["L0_attn_o_norm_w"],
        "ln1w": inputs["L0_ffn_ln1_weight"], "ln1b": inputs["L0_ffn_ln1_bias"],
        "ln2w": inputs["L0_ffn_ln2_weight"], "ln2b": inputs["L0_ffn_ln2_bias"],
        "qkvw": np.ascontiguousarray(inputs["L0_ffn_qkv_weight"].T),
        "projw": np.ascontiguousarray(inputs["L0_ffn_proj_weight"].T),
        "rw": inputs["L0_ffn_router_weight"],
        "mw1": inputs["L0_ffn_moe_w1"], "mw2": inputs["L0_ffn_moe_w2"],
    }
    return {k: np.ascontiguousarray(np.asarray(v, np.float32)) for k, v in m.items()}


def kernel(**inputs):
    nc = build_nc("full")
    in_maps = [host_inputs(inputs, cc) for cc in range(NCORES)]
    res = run_bass_kernel_spmd(nc, in_maps, core_ids=list(range(NCORES)))
    outs = []
    for cc in range(NCORES):
        a = res.results[cc]["outfm"]
        a = a.reshape(2, 8, 128, D).transpose(1, 0, 2, 3).reshape(B, TOWN, D)
        outs.append(a)
    return np.ascontiguousarray(np.concatenate(outs, axis=1), np.float32)


def build_phase1bc(kb, c, dram, scr):
    nc, p = kb.nc, kb.p
    for nm in ("spf", "sqt", "sv1", "sv0", "cv21", "cv20", "cv1", "cv0",
               "ql1", "ql0", "rows_bf", "rows",
               "gmid1", "w_ar", "xbf", "x_ar", "h_ar"):
        kb.free(nm)

    ow_sb = kb.sbuf("ow_sb", [128, 16 * 2048], BF16)
    gw2 = kb.sbuf("gw2", [128, 16 * 128], BF16)
    zero_bf = kb.sbuf("zero_bf", [128, 128], BF16)
    qh = kb.sbuf("qh", [128, TSC], BF16)
    kh = kb.sbuf("kh", [128, TSC], BF16)
    vh = kb.sbuf("vh", [128, TSC], BF16)
    Gh = kb.sbuf("Gh", [128, TSC], F32)
    Gt1 = kb.sbuf("Gt1", [128, TSC], F32)
    Gt2 = kb.sbuf("Gt2", [128, TSC], F32)
    E0 = kb.sbuf("E0", [128, TSC], BF16)
    Ece = kb.sbuf("Ece", [128, TSC], BF16)
    Er = kb.sbuf("Er", [128, TSC], BF16)
    Ec = kb.sbuf("Ec", [128, TSC], BF16)
    kbar = kb.sbuf("kbar", [128, TSC], BF16)
    qbar = kb.sbuf("qbar", [128, TSC], BF16)
    khat = kb.sbuf("khat", [128, TSC], BF16)
    ktl = kb.sbuf("ktl", [128, TSC], BF16)
    qtl = kb.sbuf("qtl", [128, TSC], BF16)
    kap = kb.sbuf("kap", [128, TSC], BF16)
    PT = kb.sbuf("PT", [128, TSC], BF16)
    WT = kb.sbuf("WT", [128, TSC], BF16)
    Pl = kb.sbuf("Pl", [128, TSC], BF16)
    R8 = kb.sbuf("R8", [128, 48], F32)
    dcol = kb.sbuf("dcol", [128, NCH], F32)
    chT = kb.sbuf("chT", [128, 7 * 128], BF16)
    sq0 = kb.sbuf("sq0", [128, 128], BF16)
    sq1 = kb.sbuf("sq1", [128, 128], BF16)
    Vt = kb.sbuf("Vt", [128, 128], BF16)
    kT = kb.sbuf("kT", [128, 128], BF16)
    rw0 = kb.sbuf("rw0", [128, 128], BF16)
    rw1 = kb.sbuf("rw1", [128, 128], BF16)
    ub = kb.sbuf("ub", [128, 128], BF16)
    Zf = kb.sbuf("Zf", [128, 128], F32)
    Zb = kb.sbuf("Zb", [128, 128], BF16)
    o_ar = kb.sbuf("o_ar", [128, H * 2 * 128], BF16)
    gate_bf = kb.sbuf("gate_bf", [128, TOWN], BF16)
    osq = kb.sbuf("osq", [128, TOWN], BF16)
    orow = kb.sbuf("orow", [1, 2 * 512], F32)
    orow_bf = kb.sbuf("orow_bf", [1, 512], BF16)
    xown = kb.sbuf("xown", [128, TOWN], F32)
    x1t = kb.sbuf("x1t", [128, TOWN], F32)

    ps_a, ps_b = kb.ps["a"], kb.ps["b"]
    ps_bc, ps_bc2 = kb.ps["bc"], kb.ps["bc2"]
    ps_l2, ps_sm = kb.ps["l2"], kb.ps["sm"]
    ps_smb = ps_sm.ap().bitcast(BF16)[:, 0:128]

    p.op("pool", lambda e: e.memset(zero_bf[:], 0.0), writes=["zero_bf"])
    for k in range(16):
        p.op("pool", lambda e, k=k: e.dma_start(
            out=ow_sb[:, k * 2048:(k + 1) * 2048],
            in_=bass.AP(dram["o_w"], k * 128 * 2048, [[2048, 128], [1, 2048]])),
            dma=True, writes=[f"ow_{k}"])
        p.op("pool", lambda e, k=k: e.dma_start(
            out=gw2[:, k * 128:(k + 1) * 128],
            in_=bass.AP(dram["g_w2"], k * 128, [[2048, 128], [1, 128]])),
            dma=True, writes=["gw2"])

    SCALES = ((16, "mT_16"), (32, "mT_32"), (64, "mT_64"), (128, "mT_128"))

    for b in range(B):
        for h in range(H):
            bh = (b * H + h) * 128 * TSC
            for nm, t, key in (("qscr", qh, "qh"), ("kscr", kh, "kh"),
                               ("vscr", vh, "vh"), ("gscr", Gh, "Gh")):
                p.op("sp", lambda e, nm=nm, t=t, bh=bh: e.dma_start(
                    out=t[:], in_=bass.AP(scr[nm], bh, [[TSC, 128], [1, TSC]])),
                    dma=True, writes=[key], tag=f"ld{nm}{b}_{h}")
            p.op("act", lambda e: e.activation(E0[:], Gh[:], AF.Exp),
                 reads=["Gh"], writes=["E0"])
            p.op("dve", lambda e: e.tensor_tensor(kbar[:], kh[:], E0[:], ALU.mult),
                 reads=["kh", "E0"], writes=["kbar"])
            p.op("dve", lambda e: e.tensor_tensor(qbar[:], qh[:], E0[:], ALU.mult),
                 reads=["qh", "E0"], writes=["qbar"])
            p.op("dve", lambda e: e.tensor_tensor(
                Gt1[:], bass.AP(Gh, 127, [[TSC, 128], [128, NCH], [0, 128]]),
                Gh[:], ALU.subtract),
                reads=["Gh"], writes=["Gt1"])
            p.op("act", lambda e: e.activation(Ece[:], Gt1[:], AF.Exp),
                 reads=["Gt1"], writes=["Ece"])
            p.op("dve", lambda e: e.tensor_tensor(khat[:], kh[:], Ece[:], ALU.mult),
                 reads=["kh", "Ece"], writes=["khat"])
            p.op("act", lambda e: e.activation(
                dcol[:], bass.AP(Gh, 127, [[TSC, 128], [128, NCH]]), AF.Exp),
                reads=["Gh"], writes=["dcol"])
            for fam, (lm, rm), m8key in (("PT", ("kap", "ktl"), "m8T_s"),
                                         ("WT", ("kap", "qtl"), "m8T_i")):
                dst = PT if fam == "PT" else WT
                p.op("dve", lambda e: e.tensor_tensor(
                    R8[:], bass.AP(Gh, 0, [[TSC, 128], [8, 48]]),
                    bass.AP(Gh, 7, [[TSC, 128], [8, 48]]), ALU.add),
                    reads=["Gh"], writes=["R8"])
                p.op("dve", lambda e: e.tensor_scalar(R8[:], R8[:], 0.5, None,
                                                      ALU.mult),
                    reads=["R8"], writes=["R8"])
                p.op("dve", lambda e: e.tensor_tensor(
                    Gt1[:], Gh[:], bass.AP(R8, 0, [[48, 128], [1, 48], [0, 8]]),
                    ALU.subtract),
                    reads=["Gh", "R8"], writes=["Gt1"])
                p.op("act", lambda e: e.activation(Er[:], Gt1[:], AF.Exp),
                     reads=["Gt1"], writes=["Er"])
                p.op("act", lambda e: e.activation(Ec[:], Gt1[:], AF.Exp,
                                                   scale=-1.0),
                     reads=["Gt1"], writes=["Ec"])
                p.op("dve", lambda e, rm=rm: e.tensor_tensor(
                    (ktl if rm == "ktl" else qtl)[:],
                    (kh if rm == "ktl" else qh)[:], Er[:], ALU.mult),
                    reads=["kh" if rm == "ktl" else "qh", "Er"], writes=[rm])
                p.op("dve", lambda e: e.tensor_tensor(kap[:], kh[:], Ec[:],
                                                      ALU.mult),
                     reads=["kh", "Ec"], writes=["kap"])
                for ci in range(NCH):
                    p.op("pe", lambda e, ci=ci, rm=rm: e.matmul(
                        ps_a[:, 0:128], kap[:, ci * 128:(ci + 1) * 128],
                        (ktl if rm == "ktl" else qtl)[:, ci * 128:(ci + 1) * 128],
                        start=True, stop=True),
                        reads=["kap", rm], writes=["ps_a"])
                    p.op("dve", lambda e, ci=ci, dst=dst, m8key=m8key: e.select(
                        dst[:, ci * 128:(ci + 1) * 128], c[m8key][:],
                        ps_a[:, 0:128], zero_bf[:]),
                        reads=["ps_a", m8key, "zero_bf"], writes=[fam + str(ci)])
                for lg, mkey in SCALES:
                    half = lg // 2
                    p.op("dve", lambda e, lg=lg, half=half: e.tensor_tensor(
                        Gt1[:], Gh[:],
                        bass.AP(Gh, half - 1, [[TSC, 128], [lg, TSC // lg], [0, lg]]),
                        ALU.subtract),
                        reads=["Gh"], writes=["Gt1"])
                    p.op("dve", lambda e: e.tensor_scalar_min(Gt2[:], Gt1[:], 0.0),
                         reads=["Gt1"], writes=["Gt2"])
                    p.op("act", lambda e: e.activation(Er[:], Gt2[:], AF.Exp),
                         reads=["Gt2"], writes=["Er"])
                    p.op("dve", lambda e: e.tensor_scalar_max(Gt2[:], Gt1[:], 0.0),
                         reads=["Gt1"], writes=["Gt2"])
                    p.op("act", lambda e: e.activation(Ec[:], Gt2[:], AF.Exp,
                                                       scale=-1.0),
                         reads=["Gt2"], writes=["Ec"])
                    p.op("dve", lambda e, rm=rm: e.tensor_tensor(
                        (ktl if rm == "ktl" else qtl)[:],
                        (kh if rm == "ktl" else qh)[:], Er[:], ALU.mult),
                        reads=["kh" if rm == "ktl" else "qh", "Er"], writes=[rm])
                    p.op("dve", lambda e: e.tensor_tensor(kap[:], kh[:], Ec[:],
                                                          ALU.mult),
                         reads=["kh", "Ec"], writes=["kap"])
                    for ci in range(NCH):
                        p.op("pe", lambda e, ci=ci, rm=rm: e.matmul(
                            ps_a[:, 0:128], kap[:, ci * 128:(ci + 1) * 128],
                            (ktl if rm == "ktl" else qtl)[:, ci * 128:(ci + 1) * 128],
                            start=True, stop=True),
                            reads=["kap", rm], writes=["ps_a"])
                        p.op("dve", lambda e, ci=ci, dst=dst, mkey=mkey:
                             e.copy_predicated(
                                 dst[:, ci * 128:(ci + 1) * 128], c[mkey][:],
                                 ps_a[:, 0:128]),
                             reads=["ps_a", mkey, fam + str(ci)],
                             writes=[fam + str(ci)])
            for ci in range(NCH):
                p.op("dve", lambda e, ci=ci, b=b, h=h: e.tensor_scalar(
                    PT[:, ci * 128:(ci + 1) * 128],
                    PT[:, ci * 128:(ci + 1) * 128],
                    c["betaT"][:, (b * NCH + ci) * 16 + h:(b * NCH + ci) * 16 + h + 1],
                    None, ALU.mult),
                    reads=["PT" + str(ci), "betaT"], writes=["PT" + str(ci)],
                    strict=True)
                p.op("pe", lambda e, ci=ci: e.transpose(
                    ps_smb, PT[:, ci * 128:(ci + 1) * 128], c["eye"][:]),
                    reads=["PT" + str(ci), "eye"], writes=["ps_sm"])
                p.op("dve", lambda e, ci=ci: e.tensor_copy(
                    Pl[:, ci * 128:(ci + 1) * 128], ps_smb),
                    reads=["ps_sm"], writes=["Pl" + str(ci)])
            for ci in range(NCH):
                sl = slice(ci * 128, (ci + 1) * 128)
                p.op("pe", lambda e, sl=sl: e.transpose(ps_smb, vh[:, sl],
                                                        c["eye"][:]),
                     reads=["vh", "eye"], writes=["ps_sm"])
                p.op("dve", lambda e: e.tensor_copy(Vt[:], ps_smb),
                     reads=["ps_sm"], writes=["Vt"])
                p.op("pe", lambda e, sl=sl: e.transpose(ps_smb, khat[:, sl],
                                                        c["eye"][:]),
                     reads=["khat", "eye"], writes=["ps_sm"])
                p.op("dve", lambda e: e.tensor_copy(kT[:], ps_smb),
                     reads=["ps_sm"], writes=["kT"])
                if ci == 0:
                    p.op("dve", lambda e: e.tensor_copy(rw0[:], Vt[:]),
                         reads=["Vt"], writes=["rw0"])
                else:
                    p.op("pe", lambda e, sl=sl: e.matmul(
                        ps_bc[:, 0:128], kbar[:, sl], Zb[:], start=True, stop=True),
                        reads=["kbar", "Zb"], writes=["ps_bc"])
                    p.op("dve", lambda e: e.tensor_tensor(
                        rw0[:], Vt[:], ps_bc[:, 0:128], ALU.subtract),
                        reads=["Vt", "ps_bc"], writes=["rw0"])
                p.op("dve", lambda e, sl=sl: e.tensor_copy(chT[:, 0:128], PT[:, sl]),
                     reads=["PT" + str(ci)], writes=["chT"])
                p.op("dve", lambda e, sl=sl: e.tensor_copy(sq0[:], Pl[:, sl]),
                     reads=["Pl" + str(ci)], writes=["sq0"])
                for lvl in range(1, 7):
                    curT = (lambda l=lvl: chT[:, (l - 1) * 128:l * 128])()
                    p.op("pe", lambda e, curT=curT: e.matmul(
                        ps_b[:, 0:128], sq0[:], curT, start=True, stop=True),
                        reads=["sq0", "chT"], writes=["ps_b"])
                    p.op("dve", lambda e, lvl=lvl: e.tensor_copy(
                        chT[:, lvl * 128:(lvl + 1) * 128], ps_b[:, 0:128]),
                        reads=["ps_b"], writes=["chT"])
                    if lvl < 6:
                        p.op("pe", lambda e, curT=curT: e.matmul(
                            ps_b[:, 128:256], curT, sq0[:], start=True, stop=True),
                            reads=["sq0", "chT"], writes=["ps_b"])
                        p.op("dve", lambda e: e.tensor_copy(sq1[:], ps_b[:, 128:256]),
                             reads=["ps_b"], writes=["sq1"])
                        p.op("dve", lambda e: e.tensor_copy(sq0[:], sq1[:]),
                             reads=["sq1"], writes=["sq0"])
                cur, curk = rw0, "rw0"
                oth, othk = rw1, "rw1"
                for lvl in range(6, -1, -1):
                    p.op("pe", lambda e, lvl=lvl, cur=cur: e.matmul(
                        ps_b[:, 256:384], chT[:, lvl * 128:(lvl + 1) * 128],
                        cur[:], start=True, stop=True),
                        reads=["chT", curk], writes=["ps_b"])
                    op = ALU.add if lvl > 0 else ALU.subtract
                    p.op("dve", lambda e, cur=cur, oth=oth, op=op: e.tensor_tensor(
                        oth[:], cur[:], ps_b[:, 256:384], op),
                        reads=[curk, "ps_b"], writes=[othk])
                    cur, oth = oth, cur
                    curk, othk = othk, curk
                p.op("dve", lambda e, b=b, h=h, ci=ci, cur=cur: e.tensor_scalar(
                    ub[:], cur[:],
                    c["betaT"][:, (b * NCH + ci) * 16 + h:(b * NCH + ci) * 16 + h + 1],
                    None, ALU.mult),
                    reads=[curk, "betaT"], writes=["ub"], strict=True)
                if ci >= 1:
                    p.op("pe", lambda e, sl=sl: e.matmul(
                        ps_bc2[:, 0:128], ub[:], WT[:, sl], start=True, stop=False),
                        reads=["ub", "WT" + str(ci)], writes=["ps_bc2"])
                    p.op("pe", lambda e, sl=sl: e.matmul(
                        ps_bc2[:, 0:128], Zb[:], qbar[:, sl], start=False, stop=True),
                        reads=["Zb", "qbar"], writes=["ps_bc2"])
                    p.op("dve", lambda e, h=h, ci=ci: e.tensor_copy(
                        o_ar[:, (h * 2 + ci - 1) * 128:(h * 2 + ci) * 128],
                        ps_bc2[:, 0:128]),
                        reads=["ps_bc2"], writes=[f"o{h}_{ci}"])
                if ci < NCH - 1:
                    p.op("pe", lambda e: e.matmul(
                        ps_bc[:, 128:256], kT[:], ub[:], start=True, stop=True),
                        reads=["kT", "ub"], writes=["ps_bc"])
                    if ci == 0:
                        p.op("dve", lambda e: e.tensor_copy(Zf[:], ps_bc[:, 128:256]),
                             reads=["ps_bc"], writes=["Zf"])
                    else:
                        p.op("dve", lambda e, ci=ci: e.tensor_scalar(
                            Zf[:], Zf[:], dcol[:, ci:ci + 1], None, ALU.mult),
                            reads=["Zf", "dcol"], writes=["Zf"], strict=True)
                        p.op("dve", lambda e: e.tensor_tensor(
                            Zf[:], Zf[:], ps_bc[:, 128:256], ALU.add),
                            reads=["Zf", "ps_bc"], writes=["Zf"])
                    p.op("dve", lambda e: e.tensor_copy(Zb[:], Zf[:]),
                         reads=["Zf"], writes=["Zb"])
            goff = b * TIN + CPAD + HALO
            p.op("pe", lambda e, h=h, goff=goff: e.matmul(
                ps_bc2[:, 128:384], gw2[:, h * 128:(h + 1) * 128],
                c["gmid2"][:, goff:goff + TOWN], start=True, stop=True),
                reads=["gw2", "gmid2_all"], writes=["ps_bc2"])
            p.op("act", lambda e, h=h: e.activation(
                gate_bf[:], ps_bc2[:, 128:384], AF.Sigmoid,
                bias=c["gb2"][:, h:h + 1]),
                reads=["ps_bc2", "gb2"], writes=["gate_bf"])
            p.op("act", lambda e, h=h: e.activation(
                osq[:], o_ar[:, h * 256:(h + 1) * 256], AF.Square),
                reads=[f"o{h}_1", f"o{h}_2"], writes=["osq"])
            p.op("pe", lambda e: e.matmul(ps_l2[0:1, 0:TOWN], c["ones_col"][:, 0:1],
                                          osq[:], start=True, stop=True),
                 reads=["osq", "ones_col"], writes=["ps_l2"])
            p.op("act", lambda e: e.activation(orow[0:1, 0:TOWN], ps_l2[0:1, 0:TOWN],
                                               AF.Sqrt, bias=c["eps5"][0:1, 0:1],
                                               scale=1.0 / DV),
                 reads=["ps_l2", "eps5"], writes=["orow"])
            p.op("dve", lambda e: e.reciprocal(orow[0:1, 512:512 + TOWN],
                                               orow[0:1, 0:TOWN]),
                 reads=["orow"], writes=["orow"])
            p.op("dve", lambda e: e.tensor_copy(orow_bf[0:1, 0:TOWN],
                                                orow[0:1, 512:512 + TOWN]),
                 reads=["orow"], writes=["orow_bf"])
            p.op("pe", lambda e: e.matmul(ps_bc[:, 256:512], c["ones_row"][:],
                                          orow_bf[0:1, 0:TOWN], start=True,
                                          stop=True),
                 reads=["orow_bf", "ones_row"], writes=["ps_bc"])
            p.op("dve", lambda e, h=h: e.tensor_tensor(
                o_ar[:, h * 256:(h + 1) * 256], o_ar[:, h * 256:(h + 1) * 256],
                ps_bc[:, 256:512], ALU.mult),
                reads=[f"o{h}_1", f"o{h}_2", "ps_bc"], writes=[f"of{h}"])
            p.op("dve", lambda e, h=h: e.scalar_tensor_tensor(
                o_ar[:, h * 256:(h + 1) * 256], o_ar[:, h * 256:(h + 1) * 256],
                c["onw"][:, 0:1], gate_bf[:], ALU.mult, ALU.mult),
                reads=[f"of{h}", "onw", "gate_bf"], writes=[f"of{h}"])
        for m in range(16):
            for h in range(H):
                p.op("pe", lambda e, m=m, h=h: e.matmul(
                    ps_a[:, 128:384], ow_sb[:, h * 2048 + m * 128:h * 2048 + m * 128 + 128],
                    o_ar[:, h * 256:(h + 1) * 256], start=(h == 0), stop=(h == 15)),
                    reads=[f"ow_{h}", f"of{h}"], writes=["ps_a"])
            p.op("sp", lambda e, b=b, m=m: e.dma_start(
                out=xown[:], in_=bass.AP(dram["xin"],
                                         (b * 16 + m) * 128 * TIN + CPAD + HALO,
                                         [[TIN, 128], [1, TOWN]])),
                dma=True, writes=["xown"], tag=f"xo{b}_{m}")
            p.op("dve", lambda e: e.tensor_tensor(x1t[:], xown[:], ps_a[:, 128:384],
                                                  ALU.add),
                 reads=["xown", "ps_a"], writes=["x1t"])
            p.op("sp", lambda e, b=b, m=m: e.dma_start(
                out=bass.AP(scr["x1scr"], (b * 16 + m) * 128 * TOWN,
                            [[TOWN, 128], [1, TOWN]]),
                in_=x1t[:]),
                dma=True, reads=["x1t"], tag=f"x1st{b}_{m}")


def build_phase2(kb, c, dram, scr):
    nc, p = kb.nc, kb.p
    for nm in ("x1t", "xown", "orow_bf", "orow", "osq", "gate_bf", "o_ar",
               "Zb", "Zf", "ub", "rw1", "rw0", "kT", "Vt", "sq1", "sq0", "chT",
               "dcol", "R8", "Pl", "WT", "PT", "kap", "qtl", "ktl", "khat",
               "qbar", "kbar", "Ec", "Er", "Ece", "E0", "Gt2", "Gt1", "Gh",
               "vh", "kh", "qh", "zero_bf", "gw2", "ow_sb",
               "betaT", "beta_bf", "beta_f", "gmid2",
               "bw", "fw2", "gw1", "fw1", "rmask"):
        kb.free(nm)

    TB = 1024
    HDI = float(HD) ** -0.5
    han = kb.sbuf("han", [128, 16 * TB], BF16)
    ln2 = kb.sbuf("ln2", [128, 16 * TB], BF16)
    gates = kb.sbuf("gates", [128, 64], F32)
    rw_sb = kb.sbuf("rw_sb", [128, 16 * 8], BF16)
    grow = kb.sbuf("grow", [128, 8], F32)
    grow2 = kb.sbuf("grow2", [128, 8], F32)
    gcol = kb.sbuf("gcol", [128, 8], F32)
    rows = kb.sbuf("p2rows", [1, 2 * 512], F32)
    rows_bf = kb.sbuf("p2rows_bf", [1, 512], BF16)
    mbc = kb.sbuf("mbc", [128, TB], BF16)
    rbc = kb.sbuf("rbc", [128, TB], BF16)
    csbc = kb.sbuf("csbc", [128, TB], BF16)
    xb = kb.sbuf("p2xb", [128, TB], BF16)
    tA = kb.sbuf("p2tA", [128, TB], F32)
    tB = kb.sbuf("p2tB", [128, TB], BF16)
    wsl = kb.sbuf("wsl", [128, 16 * 512], BF16)

    ps_a, ps_b, ps_bc, ps_bc2 = kb.ps["a"], kb.ps["b"], kb.ps["bc"], kb.ps["bc2"]
    ps_sa, ps_sb, ps_l2, ps_sm = kb.ps["sa"], kb.ps["sb"], kb.ps["l2"], kb.ps["sm"]
    ps_smb = ps_sm.ap().bitcast(BF16)[:, 0:128]
    PS4 = [ps_a, ps_b, ps_bc, ps_bc2]
    PK4 = ["ps_a", "ps_b", "ps_bc", "ps_bc2"]

    for k in range(16):
        p.op("pool", lambda e, k=k: e.dma_start(
            out=rw_sb[:, k * 8:(k + 1) * 8],
            in_=bass.AP(dram["rw"], k * 128 * 8, [[8, 128], [1, 8]])),
            dma=True, writes=["rw_sb"])
    for ei in range(E):
        for k in range(16):
            p.op("pool", lambda e, ei=ei, k=k: e.dma_start(
                out=bass.AP(scr["mw1b"], (ei * 16 + k) * 128 * FFN,
                            [[FFN, 128], [1, FFN]]),
                in_=bass.AP(dram["mw1"], (ei * 2048 + k * 128) * FFN,
                            [[FFN, 128], [1, FFN]])),
                dma=True, writes=[f"mw1b_{ei}_{k}"])
        for k in range(32):
            p.op("pool", lambda e, ei=ei, k=k: e.dma_start(
                out=bass.AP(scr["mw2b"], (ei * 32 + k) * 128 * D,
                            [[D, 128], [1, D]]),
                in_=bass.AP(dram["mw2"], (ei * FFN + k * 128) * D,
                            [[D, 128], [1, D]])),
                dma=True, writes=[f"mw2b_{ei}_{k}"])
    for k in range(16):
        p.op("pool", lambda e, k=k: e.dma_start(
            out=bass.AP(scr["qkvwb"], k * 128 * 3072, [[3072, 128], [1, 3072]]),
            in_=bass.AP(dram["qkvw"], k * 128 * 3072, [[3072, 128], [1, 3072]])),
            dma=True, writes=[f"qkvwb_{k}"])
        p.op("pool", lambda e, k=k: e.dma_start(
            out=bass.AP(scr["projwb"], k * 128 * D, [[D, 128], [1, D]]),
            in_=bass.AP(dram["projw"], k * 128 * D, [[D, 128], [1, D]])),
            dma=True, writes=[f"projwb_{k}"])

    def load_x1_tile(tb, m, dst, dstkey, cast):
        eng = "pool" if cast else "sp"
        for s in range(8):
            p.op(eng, lambda e, m=m, s=s, tb=tb: e.dma_start(
                out=dst[:, s * 128:(s + 1) * 128],
                in_=bass.AP(scr["x1scr"], (s * 16 + m) * 128 * TOWN + tb * 128,
                            [[TOWN, 128], [1, 128]])),
                dma=True, writes=[dstkey], tag=f"x1l{tb}_{m}_{s}")

    for tb in range(2):
        qkvb = kb.sbuf("qkvb", [128, 24 * TB], BF16)
        for m in range(16):
            load_x1_tile(tb, m, xb, "p2xb", True)
            p.op("act", lambda e: e.activation(tB[:], xb[:], AF.Square),
                 reads=["p2xb"], writes=["p2tB"])
            for sg in range(2):
                pssx = ps_sa if sg == 0 else ps_l2
                pssq = ps_sb if sg == 0 else ps_bc
                kx = "ps_sa" if sg == 0 else "ps_l2"
                kq = "ps_sb" if sg == 0 else "ps_bc"
                p.op("pe", lambda e, m=m, sg=sg, pssx=pssx: e.matmul(
                    pssx[0:1, 0:512], c["ones_col"][:, 0:1],
                    xb[:, sg * 512:(sg + 1) * 512],
                    start=(m == 0), stop=(m == 15)),
                    reads=["p2xb", "ones_col"], writes=[kx])
                p.op("pe", lambda e, m=m, sg=sg, pssq=pssq: e.matmul(
                    pssq[0:1, 0:512], c["ones_col"][:, 0:1],
                    tB[:, sg * 512:(sg + 1) * 512],
                    start=(m == 0), stop=(m == 15)),
                    reads=["p2tB", "ones_col"], writes=[kq])
        for sg in range(2):
            pssx = ps_sa if sg == 0 else ps_l2
            pssq = ps_sb if sg == 0 else ps_bc
            kx = "ps_sa" if sg == 0 else "ps_l2"
            kq = "ps_sb" if sg == 0 else "ps_bc"
            R0 = rows[0:1, 0:512]
            R1 = rows[0:1, 512:1024]
            p.op("dve", lambda e, pssx=pssx: e.tensor_scalar(
                R0, pssx[0:1, 0:512], 1.0 / D, None, ALU.mult),
                reads=[kx], writes=["p2rows"])
            p.op("act", lambda e: e.activation(R1, R0, AF.Square),
                 reads=["p2rows"], writes=["p2rows"])
            p.op("dve", lambda e, pssq=pssq: e.scalar_tensor_tensor(
                R1, pssq[0:1, 0:512], 1.0 / D, R1, ALU.mult, ALU.subtract),
                reads=[kq, "p2rows"], writes=["p2rows"])
            p.op("dve", lambda e: e.tensor_copy(rows_bf[0:1, 0:512], R0),
                 reads=["p2rows"], writes=["p2rows_bf"])
            p.op("pe", lambda e: e.matmul(ps_sm[:, 0:128] if False else
                                          ps_sa[:, 0:512], c["ones_row"][:],
                                          rows_bf[0:1, 0:512], start=True,
                                          stop=True),
                 reads=["p2rows_bf", "ones_row"], writes=["ps_sa"])
            p.op("dve", lambda e, sg=sg: e.tensor_copy(
                mbc[:, sg * 512:(sg + 1) * 512], ps_sa[:, 0:512]),
                reads=["ps_sa"], writes=["mbc"])
            p.op("act", lambda e: e.activation(R0, R1, AF.Sqrt,
                                               bias=c["eps5"][0:1, 0:1]),
                 reads=["p2rows", "eps5"], writes=["p2rows"])
            p.op("dve", lambda e: e.reciprocal(R0, R0),
                 reads=["p2rows"], writes=["p2rows"])
            p.op("dve", lambda e: e.tensor_copy(rows_bf[0:1, 0:512], R0),
                 reads=["p2rows"], writes=["p2rows_bf"])
            p.op("pe", lambda e: e.matmul(ps_sa[:, 0:512], c["ones_row"][:],
                                          rows_bf[0:1, 0:512], start=True,
                                          stop=True),
                 reads=["p2rows_bf", "ones_row"], writes=["ps_sa"])
            p.op("dve", lambda e, sg=sg: e.tensor_copy(
                rbc[:, sg * 512:(sg + 1) * 512], ps_sa[:, 0:512]),
                reads=["ps_sa"], writes=["rbc"])
            p.op("dve", lambda e: e.tensor_tensor(
                rows[0:1, 512:1024], R0, R0, ALU.mult),
                reads=["p2rows"], writes=["p2rows"])
            p.op("dve", lambda e, pssq=pssq: e.scalar_tensor_tensor(
                rows[0:1, 512:1024], rows[0:1, 512:1024], 1.0,
                rows[0:1, 512:1024], ALU.mult, ALU.mult) if False else
                e.tensor_tensor(rows[0:1, 512:1024], rows[0:1, 512:1024],
                                rows[0:1, 512:1024], ALU.bypass),
                reads=["p2rows"], writes=["p2rows"])
            p.op("dve", lambda e: e.tensor_scalar(
                rows[0:1, 512:1024], rows[0:1, 512:1024], 1e-5, None,
                ALU.mult),
                reads=["p2rows"], writes=["p2rows"])
            p.op("dve", lambda e: e.tensor_scalar(
                rows[0:1, 512:1024], rows[0:1, 512:1024], -1.0, 1.0,
                ALU.mult, ALU.add),
                reads=["p2rows"], writes=["p2rows"])
            p.op("act", lambda e: e.activation(rows[0:1, 512:1024],
                                               rows[0:1, 512:1024], AF.Sqrt,
                                               bias=c["eps5"][0:1, 0:1]),
                 reads=["p2rows", "eps5"], writes=["p2rows"])
            p.op("dve", lambda e: e.reciprocal(rows[0:1, 512:1024],
                                               rows[0:1, 512:1024]),
                 reads=["p2rows"], writes=["p2rows"])
            p.op("dve", lambda e: e.tensor_tensor(R0, R0, rows[0:1, 512:1024],
                                                  ALU.mult),
                 reads=["p2rows"], writes=["p2rows"])
            p.op("dve", lambda e: e.tensor_copy(rows_bf[0:1, 0:512], R0),
                 reads=["p2rows"], writes=["p2rows_bf"])
            p.op("pe", lambda e: e.matmul(ps_sa[:, 0:512], c["ones_row"][:],
                                          rows_bf[0:1, 0:512], start=True,
                                          stop=True),
                 reads=["p2rows_bf", "ones_row"], writes=["ps_sa"])
            p.op("dve", lambda e, sg=sg: e.tensor_copy(
                csbc[:, sg * 512:(sg + 1) * 512], ps_sa[:, 0:512]),
                reads=["ps_sa"], writes=["csbc"])
        ln1 = kb.sbuf("ln1", [128, 16 * TB], BF16)
        for m in range(16):
            load_x1_tile(tb, m, xb, "p2xb", True)
            p.op("dve", lambda e: e.tensor_tensor(tA[:], xb[:], mbc[:],
                                                  ALU.subtract),
                 reads=["p2xb", "mbc"], writes=["p2tA"])
            p.op("dve", lambda e, m=m: e.tensor_tensor(
                han[:, m * TB:(m + 1) * TB], tA[:], rbc[:], ALU.mult),
                reads=["p2tA", "rbc"], writes=[f"han{m}"])
            p.op("dve", lambda e: e.tensor_tensor(tB[:], tA[:], csbc[:],
                                                  ALU.mult),
                 reads=["p2tA", "csbc"], writes=["p2tB"])
            p.op("act", lambda e, m=m: e.activation(
                ln1[:, m * TB:(m + 1) * TB], tB[:], AF.Identity,
                bias=c["c_ln1b"][:, m:m + 1], scale=c["c_ln1w"][:, m:m + 1]),
                reads=["p2tB", "c_ln1b", "c_ln1w"], writes=[f"ln1_{m}"])
        for mg in range(12):
            for k in range(16):
                eng = "sp" if k % 2 == 0 else "act"
                p.op(eng, lambda e, mg=mg, k=k: e.dma_start(
                    out=wsl[:, k * 512:k * 512 + 256],
                    in_=bass.AP(scr["qkvwb"], k * 128 * 3072 + mg * 256,
                                [[3072, 128], [1, 256]])),
                    dma=True, reads=[f"qkvwb_{k}"], writes=[f"wsl_{k}"],
                    tag=f"qw{tb}_{mg}_{k}")
            for ml in range(2):
                for n in range(2):
                    pi_ = ml * 2 + n
                    for k in range(16):
                        p.op("pe", lambda e, mg=mg, ml=ml, n=n, k=k, pi_=pi_:
                             e.matmul(
                                 PS4[pi_][:, 0:512],
                                 wsl[:, k * 512 + ml * 128:k * 512 + ml * 128 + 128],
                                 ln1[:, (k * TB) + n * 512:(k * TB) + n * 512 + 512],
                                 start=(k == 0), stop=(k == 15)),
                             reads=[f"wsl_{k}", f"ln1_{k}"], writes=[PK4[pi_]])
                    p.op("dve", lambda e, mg=mg, ml=ml, n=n, pi_=pi_:
                         e.tensor_copy(
                             qkvb[:, (mg * 2 + ml) * TB + n * 512:
                                  (mg * 2 + ml) * TB + n * 512 + 512],
                             PS4[pi_][:, 0:512]),
                         reads=[PK4[pi_]], writes=[f"qkv_{mg * 2 + ml}"])
        kb.free("ln1")
        attw = kb.sbuf("attw", [128, 8 * 128], BF16)
        pP = kb.sbuf("pP", [128, 512], BF16)
        pPT = kb.sbuf("pPT", [128, 512], BF16)
        vT = kb.sbuf("vT", [128, 128], BF16)
        scf = kb.sbuf("scf", [128, 512], F32)
        mx = kb.sbuf("mx", [128, 8], F32)
        attn_fm = kb.sbuf("attn_fm", [128, 16 * TB], BF16)
        for w in range(8):
            for g in range(4):
                base = (g * 6) * TB
                for hq in range(4):
                    p.op("dve", lambda e, w=w, g=g, hq=hq: e.tensor_single_scalar(
                        attw[:, hq * 128:(hq + 1) * 128],
                        bass.AP(qkvb, (g * 6 + hq) * TB + 16 * w,
                                [[24 * TB, 128], [1, 16], [128, 8]]),
                        HDI, ALU.mult),
                        reads=[f"qkv_{g * 6 + hq}"], writes=["attw"])
                for src_i, slot in ((4, 4), (5, 5)):
                    p.op("dve", lambda e, w=w, g=g, src_i=src_i, slot=slot:
                         e.tensor_copy(
                             attw[:, slot * 128:(slot + 1) * 128],
                             bass.AP(qkvb, (g * 6 + src_i) * TB + 16 * w,
                                     [[24 * TB, 128], [1, 16], [128, 8]])),
                         reads=[f"qkv_{g * 6 + src_i}"], writes=["attw"])
                for hq in range(4):
                    p.op("pe", lambda e, hq=hq: e.matmul(
                        ps_a[:, hq * 128:(hq + 1) * 128],
                        attw[:, hq * 128:(hq + 1) * 128],
                        attw[:, 4 * 128:5 * 128],
                        start=(hq == 0), stop=(hq == 3)),
                        reads=["attw"], writes=["ps_a"])
                p.op("dve", lambda e: e.select(
                    scf[:], bass.AP(c["m8_i"], 0, [[128, 128], [0, 4], [1, 128]]),
                    ps_a[:, 0:512],
                    bass.AP(c["negbig"], 0, [[128, 128], [0, 4], [1, 128]])),
                    reads=["ps_a", "m8_i", "negbig"], writes=["scf"],
                    strict=True)
                p.op("dve", lambda e: e.reduce_max(
                    mx[:, 0:4], bass.AP(scf, 0, [[512, 128], [128, 4], [1, 128]]),
                    AX.X),
                    reads=["scf"], writes=["mx"], strict=True)
                p.op("dve", lambda e: e.tensor_single_scalar(
                    mx[:, 0:4], mx[:, 0:4], -1.0, ALU.mult),
                    reads=["mx"], writes=["mx"], strict=True)
                for hq in range(4):
                    p.op("act", lambda e, hq=hq: e.activation(
                        pP[:, hq * 128:(hq + 1) * 128],
                        scf[:, hq * 128:(hq + 1) * 128], AF.Exp,
                        bias=mx[:, hq:hq + 1]),
                        reads=["scf", "mx"], writes=["pP"])
                p.op("dve", lambda e: e.reduce_sum(
                    mx[:, 4:8], bass.AP(pP, 0, [[512, 128], [128, 4], [1, 128]]),
                    AX.X),
                    reads=["pP"], writes=["mx"], strict=True)
                p.op("dve", lambda e: e.reciprocal(mx[:, 4:8], mx[:, 4:8]),
                     reads=["mx"], writes=["mx"], strict=True)
                for hq in range(4):
                    p.op("act", lambda e, hq=hq: e.activation(
                        pP[:, hq * 128:(hq + 1) * 128],
                        pP[:, hq * 128:(hq + 1) * 128], AF.Identity,
                        scale=mx[:, 4 + hq:5 + hq]),
                        reads=["pP", "mx"], writes=["pP"])
                for hq in range(4):
                    p.op("pe", lambda e, hq=hq: e.transpose(
                        ps_smb, pP[:, hq * 128:(hq + 1) * 128], c["eye"][:]),
                        reads=["pP", "eye"], writes=["ps_sm"])
                    p.op("dve", lambda e, hq=hq: e.tensor_copy(
                        pPT[:, hq * 128:(hq + 1) * 128], ps_smb),
                        reads=["ps_sm"], writes=["pPT"])
                p.op("pe", lambda e: e.transpose(ps_smb, attw[:, 5 * 128:6 * 128],
                                                 c["eye"][:]),
                     reads=["attw", "eye"], writes=["ps_sm"])
                p.op("dve", lambda e: e.tensor_copy(vT[:], ps_smb),
                     reads=["ps_sm"], writes=["vT"])
                for hq in range(4):
                    p.op("pe", lambda e, hq=hq: e.matmul(
                        ps_b[:, hq * 128:(hq + 1) * 128], vT[:],
                        pPT[:, hq * 128:(hq + 1) * 128],
                        start=(hq == 0), stop=(hq == 3)),
                        reads=["vT", "pPT"], writes=["ps_b"])
                if DBG_P2 and tb == 0 and w == 0 and g == 2:
                    dw = nc.dram_tensor("dbg_attw", [128, 8 * 128], BF16,
                                        kind="ExternalOutput")
                    dp = nc.dram_tensor("dbg_pP", [128, 512], BF16,
                                        kind="ExternalOutput")
                    p.op("sp", lambda e: e.dma_start(out=dw.ap(), in_=attw[:]),
                         dma=True, reads=["attw"])
                    p.op("sp", lambda e: e.dma_start(out=dp.ap(), in_=pP[:]),
                         dma=True, reads=["pP"])
                for hq in range(4):
                    gh = g * 4 + hq
                    p.op("dve", lambda e, w=w, gh=gh, hq=hq: e.tensor_copy(
                        bass.AP(attn_fm, gh * TB + 16 * w,
                                [[16 * TB, 128], [1, 16], [128, 8]]),
                        bass.AP(ps_b, hq * 128, [[512, 128], [8, 16], [1, 8]])),
                        reads=["ps_b"], writes=[f"afm_{gh}"])
        for mg in range(8):
            for k in range(16):
                eng = "sp" if k % 2 == 0 else "act"
                p.op(eng, lambda e, mg=mg, k=k: e.dma_start(
                    out=wsl[:, k * 512:k * 512 + 256],
                    in_=bass.AP(scr["projwb"], k * 128 * 2048 + mg * 256,
                                [[2048, 128], [1, 256]])),
                    dma=True, reads=[f"projwb_{k}"], writes=[f"wsl_{k}"],
                    tag=f"pw{tb}_{mg}_{k}")
            for ml in range(2):
                for n in range(2):
                    pi_ = ml * 2 + n
                    for k in range(16):
                        p.op("pe", lambda e, ml=ml, n=n, k=k, pi_=pi_: e.matmul(
                            PS4[pi_][:, 0:512],
                            wsl[:, k * 512 + ml * 128:k * 512 + ml * 128 + 128],
                            attn_fm[:, k * TB + n * 512:k * TB + n * 512 + 512],
                            start=(k == 0), stop=(k == 15)),
                            reads=[f"wsl_{k}", f"afm_{k}"], writes=[PK4[pi_]])
                    mo = mg * 2 + ml
                    p.op("dve", lambda e, mo=mo, n=n, pi_=pi_: e.tensor_tensor(
                        han[:, mo * TB + n * 512:mo * TB + n * 512 + 512],
                        han[:, mo * TB + n * 512:mo * TB + n * 512 + 512],
                        PS4[pi_][:, 0:512], ALU.add),
                        reads=[f"han{mo}", PK4[pi_]], writes=[f"han{mo}"])
        if DBG_P2 and tb == 0:
            dq = nc.dram_tensor("dbg_qkvb", [128, 24 * TB], BF16,
                                kind="ExternalOutput")
            da = nc.dram_tensor("dbg_attn", [128, 16 * TB], BF16,
                                kind="ExternalOutput")
            dh = nc.dram_tensor("dbg_han", [128, 16 * TB], BF16,
                                kind="ExternalOutput")
            p.op("sp", lambda e: e.dma_start(out=dq.ap(), in_=qkvb[:]), dma=True,
                 reads=[f"qkv_{i}" for i in range(24)])
            p.op("sp", lambda e: e.dma_start(out=da.ap(), in_=attn_fm[:]),
                 dma=True, reads=[f"afm_{i}" for i in range(16)])
            p.op("sp", lambda e: e.dma_start(out=dh.ap(), in_=han[:]), dma=True,
                 reads=[f"han{i}" for i in range(16)])
        kb.free("attn_fm", "mx", "scf", "vT", "pPT", "pP", "attw", "qkvb")
        for m in range(16):
            p.op("act", lambda e, m=m: e.activation(
                tB[:], han[:, m * TB:(m + 1) * TB], AF.Square),
                reads=[f"han{m}"], writes=["p2tB"])
            for sg in range(2):
                pssx = ps_sa if sg == 0 else ps_l2
                pssq = ps_sb if sg == 0 else ps_bc
                kx = "ps_sa" if sg == 0 else "ps_l2"
                kq = "ps_sb" if sg == 0 else "ps_bc"
                p.op("pe", lambda e, m=m, sg=sg, pssx=pssx: e.matmul(
                    pssx[0:1, 0:512], c["ones_col"][:, 0:1],
                    han[:, m * TB + sg * 512:m * TB + (sg + 1) * 512],
                    start=(m == 0), stop=(m == 15)),
                    reads=[f"han{m}", "ones_col"], writes=[kx])
                p.op("pe", lambda e, m=m, sg=sg, pssq=pssq: e.matmul(
                    pssq[0:1, 0:512], c["ones_col"][:, 0:1],
                    tB[:, sg * 512:(sg + 1) * 512],
                    start=(m == 0), stop=(m == 15)),
                    reads=["p2tB", "ones_col"], writes=[kq])
        for sg in range(2):
            pssx = ps_sa if sg == 0 else ps_l2
            pssq = ps_sb if sg == 0 else ps_bc
            kx = "ps_sa" if sg == 0 else "ps_l2"
            kq = "ps_sb" if sg == 0 else "ps_bc"
            R0 = rows[0:1, 0:512]
            R1 = rows[0:1, 512:1024]
            p.op("dve", lambda e, pssx=pssx: e.tensor_scalar(
                R0, pssx[0:1, 0:512], 1.0 / D, None, ALU.mult),
                reads=[kx], writes=["p2rows"])
            p.op("act", lambda e: e.activation(R1, R0, AF.Square),
                 reads=["p2rows"], writes=["p2rows"])
            p.op("dve", lambda e, pssq=pssq: e.scalar_tensor_tensor(
                R1, pssq[0:1, 0:512], 1.0 / D, R1, ALU.mult, ALU.subtract),
                reads=[kq, "p2rows"], writes=["p2rows"])
            p.op("dve", lambda e: e.tensor_copy(rows_bf[0:1, 0:512], R0),
                 reads=["p2rows"], writes=["p2rows_bf"])
            p.op("pe", lambda e: e.matmul(ps_sa[:, 0:512], c["ones_row"][:],
                                          rows_bf[0:1, 0:512], start=True,
                                          stop=True),
                 reads=["p2rows_bf", "ones_row"], writes=["ps_sa"])
            p.op("dve", lambda e, sg=sg: e.tensor_copy(
                mbc[:, sg * 512:(sg + 1) * 512], ps_sa[:, 0:512]),
                reads=["ps_sa"], writes=["mbc"])
            p.op("act", lambda e: e.activation(R0, R1, AF.Sqrt,
                                               bias=c["eps5"][0:1, 0:1]),
                 reads=["p2rows", "eps5"], writes=["p2rows"])
            p.op("dve", lambda e: e.reciprocal(R0, R0),
                 reads=["p2rows"], writes=["p2rows"])
            p.op("dve", lambda e: e.tensor_copy(rows_bf[0:1, 0:512], R0),
                 reads=["p2rows"], writes=["p2rows_bf"])
            p.op("pe", lambda e: e.matmul(ps_sa[:, 0:512], c["ones_row"][:],
                                          rows_bf[0:1, 0:512], start=True,
                                          stop=True),
                 reads=["p2rows_bf", "ones_row"], writes=["ps_sa"])
            p.op("dve", lambda e, sg=sg: e.tensor_copy(
                rbc[:, sg * 512:(sg + 1) * 512], ps_sa[:, 0:512]),
                reads=["ps_sa"], writes=["rbc"])
        for m in range(16):
            p.op("dve", lambda e, m=m: e.tensor_tensor(
                tA[:], han[:, m * TB:(m + 1) * TB], mbc[:], ALU.subtract),
                reads=[f"han{m}", "mbc"], writes=["p2tA"])
            p.op("dve", lambda e: e.tensor_tensor(tB[:], tA[:], rbc[:], ALU.mult),
                 reads=["p2tA", "rbc"], writes=["p2tB"])
            p.op("act", lambda e, m=m: e.activation(
                ln2[:, m * TB:(m + 1) * TB], tB[:], AF.Identity,
                bias=c["c_ln2b"][:, m:m + 1], scale=c["c_ln2w"][:, m:m + 1]),
                reads=["p2tB", "c_ln2b", "c_ln2w"], writes=[f"ln2_{m}"])
        out_tm = kb.sbuf("out_tm", [128, 8 * 2048], F32)
        h1 = kb.sbuf("h1", [128, 16 * TB], BF16)
        for m in range(16):
            load_x1_tile(tb, m, xb, "p2xb", True)
            p.op("dve", lambda e, m=m: e.tensor_tensor(
                tB[:], xb[:], han[:, m * TB:(m + 1) * TB], ALU.add),
                reads=["p2xb", f"han{m}"], writes=["p2tB"])
            for s in range(8):
                p.op("pe", lambda e, s=s: e.transpose(
                    ps_smb, tB[:, s * 128:(s + 1) * 128], c["eye"][:]),
                    reads=["p2tB", "eye"], writes=["ps_sm"])
                p.op("dve", lambda e, s=s, m=m: e.tensor_copy(
                    out_tm[:, s * 2048 + m * 128:s * 2048 + (m + 1) * 128],
                    ps_smb),
                    reads=["ps_sm"], writes=[f"ot_{s}"])
        for tc in range(8):
            for k in range(16):
                p.op("pe", lambda e, tc=tc, k=k: e.matmul(
                    ps_sm[:, 0:8], ln2[:, k * TB + tc * 128:k * TB + tc * 128 + 128],
                    rw_sb[:, k * 8:(k + 1) * 8], start=(k == 0), stop=(k == 15)),
                    reads=[f"ln2_{k}", "rw_sb"], writes=["ps_sm"])
            p.op("dve", lambda e: e.tensor_copy(grow[:], ps_sm[:, 0:8]),
                 reads=["ps_sm"], writes=["grow"], strict=True)
            p.op("dve", lambda e: e.reduce_max(gcol[:, 0:1], grow[:], AX.X),
                 reads=["grow"], writes=["gcol"], strict=True)
            p.op("dve", lambda e: e.tensor_single_scalar(
                gcol[:, 4:5], gcol[:, 0:1], -1.0, ALU.mult),
                reads=["gcol"], writes=["gcol"], strict=True)
            p.op("act", lambda e: e.activation(
                grow[:], grow[:], AF.Identity, bias=gcol[:, 4:5]),
                reads=["grow", "gcol"], writes=["grow"])
            p.op("dve", lambda e: e.tensor_single_scalar(
                grow2[:], grow[:], 0.0, ALU.is_ge),
                reads=["grow"], writes=["grow2"], strict=True)
            p.op("dve", lambda e: e.scalar_tensor_tensor(
                grow2[:], grow2[:], -1e30, grow[:], ALU.mult, ALU.add),
                reads=["grow2", "grow"], writes=["grow2"], strict=True)
            p.op("dve", lambda e: e.reduce_max(gcol[:, 1:2], grow2[:], AX.X),
                 reads=["grow2"], writes=["gcol"], strict=True)
            p.op("act", lambda e: e.activation(gcol[:, 2:3], gcol[:, 1:2],
                                               AF.Exp),
                 reads=["gcol"], writes=["gcol"])
            p.op("dve", lambda e: e.tensor_single_scalar(
                gcol[:, 2:3], gcol[:, 2:3], 1.0, ALU.add),
                reads=["gcol"], writes=["gcol"], strict=True)
            p.op("dve", lambda e: e.reciprocal(gcol[:, 3:4], gcol[:, 2:3]),
                 reads=["gcol"], writes=["gcol"], strict=True)
            p.op("dve", lambda e: e.tensor_single_scalar(
                gcol[:, 5:6], gcol[:, 1:2], -1.0, ALU.mult),
                reads=["gcol"], writes=["gcol"], strict=True)
            p.op("act", lambda e: e.activation(
                grow2[:], grow[:], AF.Identity, bias=gcol[:, 5:6]),
                reads=["grow", "gcol"], writes=["grow2"])
            p.op("dve", lambda e: e.tensor_single_scalar(
                grow2[:], grow2[:], 0.0, ALU.is_ge),
                reads=["grow2"], writes=["grow2"], strict=True)
            p.op("act", lambda e: e.activation(grow[:], grow[:], AF.Exp),
                 reads=["grow"], writes=["grow"])
            p.op("dve", lambda e: e.tensor_tensor(grow[:], grow[:], grow2[:],
                                                  ALU.mult),
                 reads=["grow", "grow2"], writes=["grow"], strict=True)
            p.op("act", lambda e, tc=tc: e.activation(
                gates[:, tc * 8:(tc + 1) * 8], grow[:], AF.Identity,
                scale=gcol[:, 3:4]),
                reads=["grow", "gcol"], writes=["gates"])
        if DBG_P2 and tb == 0:
            dl = nc.dram_tensor("dbg_ln2", [128, 16 * TB], BF16,
                                kind="ExternalOutput")
            dg = nc.dram_tensor("dbg_gates", [128, 64], F32,
                                kind="ExternalOutput")
            do = nc.dram_tensor("dbg_ot0", [128, 8 * 2048], F32,
                                kind="ExternalOutput")
            p.op("sp", lambda e: e.dma_start(out=dl.ap(), in_=ln2[:]), dma=True,
                 reads=[f"ln2_{i}" for i in range(16)])
            p.op("sp", lambda e: e.dma_start(out=dg.ap(), in_=gates[:]), dma=True,
                 reads=["gates"])
            p.op("sp", lambda e: e.dma_start(out=do.ap(), in_=out_tm[:]), dma=True,
                 reads=[f"ot_{s}" for s in range(8)])
        for ei in range(E):
            for hf in range(2):
                for mg in range(8):
                    for k in range(16):
                        eng = "sp" if k % 2 == 0 else "act"
                        p.op(eng, lambda e, ei=ei, hf=hf, mg=mg, k=k: e.dma_start(
                            out=wsl[:, k * 512:k * 512 + 256],
                            in_=bass.AP(scr["mw1b"],
                                        (ei * 16 + k) * 128 * FFN + hf * 2048 + mg * 256,
                                        [[FFN, 128], [1, 256]]),),
                            dma=True, reads=[f"mw1b_{ei}_{k}"],
                            writes=[f"wsl_{k}"],
                            tag=f"m1{tb}_{ei}_{hf}_{mg}_{k}")
                    for ml in range(2):
                        for n in range(2):
                            pi_ = ml * 2 + n
                            for k in range(16):
                                p.op("pe", lambda e, ml=ml, n=n, k=k, pi_=pi_:
                                     e.matmul(
                                         PS4[pi_][:, 0:512],
                                         wsl[:, k * 512 + ml * 128:k * 512 + ml * 128 + 128],
                                         ln2[:, k * TB + n * 512:k * TB + n * 512 + 512],
                                         start=(k == 0), stop=(k == 15)),
                                     reads=[f"wsl_{k}", f"ln2_{k}"],
                                     writes=[PK4[pi_]])
                            mo = mg * 2 + ml
                            p.op("act", lambda e, mo=mo, n=n, pi_=pi_:
                                 e.activation(
                                     h1[:, mo * TB + n * 512:mo * TB + n * 512 + 512],
                                     PS4[pi_][:, 0:512], AF.Gelu),
                                 reads=[PK4[pi_]], writes=[f"h1_{mo}"])
                for hf2 in range(1):
                    pass
                for n in range(4):
                    for k2 in range(16):
                        eng = "sp" if k2 % 2 == 0 else "act"
                        p.op(eng, lambda e, ei=ei, hf=hf, n=n, k2=k2: e.dma_start(
                            out=wsl[:, k2 * 512:(k2 + 1) * 512],
                            in_=bass.AP(scr["mw2b"],
                                        (ei * 32 + hf * 16 + k2) * 128 * D + n * 512,
                                        [[D, 128], [1, 512]]),),
                            dma=True,
                            reads=[f"mw2b_{ei}_{hf * 16 + k2}"],
                            writes=[f"wsl_{k2}"],
                            tag=f"m2{tb}_{ei}_{hf}_{n}_{k2}")
                    for tc in range(8):
                        pi_ = tc % 2
                        for k2 in range(16):
                            p.op("pe", lambda e, tc=tc, k2=k2, pi_=pi_: e.matmul(
                                PS4[pi_][:, 0:512],
                                h1[:, k2 * TB + tc * 128:k2 * TB + tc * 128 + 128],
                                wsl[:, k2 * 512:(k2 + 1) * 512],
                                start=(k2 == 0), stop=(k2 == 15)),
                                reads=[f"wsl_{k2}", f"h1_{k2}"],
                                writes=[PK4[pi_]])
                        p.op("dve", lambda e, tc=tc, n=n, ei=ei, pi_=pi_:
                             e.scalar_tensor_tensor(
                                 out_tm[:, tc * 2048 + n * 512:tc * 2048 + n * 512 + 512],
                                 PS4[pi_][:, 0:512],
                                 gates[:, tc * 8 + ei:tc * 8 + ei + 1],
                                 out_tm[:, tc * 2048 + n * 512:tc * 2048 + n * 512 + 512],
                                 ALU.mult, ALU.add),
                             reads=[PK4[pi_], "gates", f"ot_{tc}"],
                             writes=[f"ot_{tc}"])
        for s in range(8):
            p.op("sp", lambda e, s=s, tb=tb: e.dma_start(
                out=bass.AP(scr["outfm"], (tb * 8 + s) * 128 * 2048,
                            [[2048, 128], [1, 2048]]),
                in_=out_tm[:, s * 2048:(s + 1) * 2048]),
                dma=True, reads=[f"ot_{s}"], tag=f"out{tb}_{s}")
        kb.free("h1", "out_tm")
